# version 09
# baseline (speedup 1.0000x reference)
import numpy as np
import ml_dtypes
from contextlib import ExitStack

import concourse.bass as bass
import concourse.mybir as mybir
from concourse.bass_utils import run_bass_kernel_spmd

F32 = mybir.dt.float32
BF16 = mybir.dt.bfloat16
AF = mybir.ActivationFunctionType
ALU = mybir.AluOpType
AXX = mybir.AxisListType.X

D = 1024
DC = 8
H = 4096
HC = 32
HEADS = 8
DH = 128
FEAT = 64
NTOK = 4096
TILES = [456] * 8 + [448]
MAXE = 462
XE = 512
VST = 130
assert sum(TILES) == NTOK
EPS_LN = 1e-5
EPS_Z = 1e-8
STAGE = 3

_CACHED = {}


class SemC:
    def __init__(self, name):
        self.name = name
        self.h = None
        self.n = 0

    def inc(self, amt=1):
        self.n += amt
        return self.n


class Rec:
    def __init__(self, eng_sems=None):
        self.ops = []
        self.eng_sems = eng_sems or {}

    def op(self, eng, fn=None, waits=(), inc=None):
        incs = ()
        if inc is not None:
            incs = tuple(inc) if isinstance(inc, list) else (inc,)
        waits = list(waits)
        own = self.eng_sems.get(eng)
        if own is not None and fn is not None:
            own_amt = sum(a for (s, a) in incs if s is own)
            if own_amt == 0:
                own.inc(1)
                incs = incs + ((own, 1),)
                own_amt = 1
            waits.append((own, own.n - own_amt))
        waits = tuple((s, v) for (s, v) in waits if v > 0)
        for s, v in waits:
            assert v <= s.n, f"wait-on-future: eng={eng} sem={s.name} v={v} planned={s.n}"
        self.ops.append((eng, fn, waits, incs))


def cinc(sem):
    sem.inc(1)
    return (sem, 1)


def build_nc(stage):
    nc = bass.Bass("TRN2", target_bir_lowering=False, debug=False)

    xT = nc.dram_tensor("xT", [D, NTOK + 6], F32, kind="ExternalInput").ap()
    wqkv = nc.dram_tensor("wqkv", [D, 3 * D], F32, kind="ExternalInput").ap()
    wout = nc.dram_tensor("wout", [D, D], F32, kind="ExternalInput").ap()
    proj = nc.dram_tensor("proj", [DH, FEAT], F32, kind="ExternalInput").ap()
    w1t = nc.dram_tensor("w1t", [D, H], F32, kind="ExternalInput").ap()
    w1n = nc.dram_tensor("w1n", [H, D], F32, kind="ExternalInput").ap()
    w2t = nc.dram_tensor("w2t", [H, D], F32, kind="ExternalInput").ap()
    w2n = nc.dram_tensor("w2n", [D, H], F32, kind="ExternalInput").ap()
    b1c = nc.dram_tensor("b1c", [128, HC], F32, kind="ExternalInput").ap()
    b2c = nc.dram_tensor("b2c", [128, DC], F32, kind="ExternalInput").ap()
    boutc = nc.dram_tensor("boutc", [128, DC], F32, kind="ExternalInput").ap()
    maskA3 = nc.dram_tensor("maskA3", [128, 3 * 128], F32, kind="ExternalInput").ap()
    maskB = nc.dram_tensor("maskB", [6, 128], F32, kind="ExternalInput").ap()
    identb = nc.dram_tensor("identb", [128, 128], BF16, kind="ExternalInput").ap()
    onesb = nc.dram_tensor("onesb", [128, 1], BF16, kind="ExternalInput").ap()
    onesr = nc.dram_tensor("onesr", [1, 128], BF16, kind="ExternalInput").ap()
    ones1 = nc.dram_tensor("ones1", [1, 128], F32, kind="ExternalInput").ap()
    cconst = nc.dram_tensor("cconst", [128, 3], F32, kind="ExternalInput").ap()
    yT = nc.dram_tensor("yT", [D, NTOK], F32, kind="ExternalOutput").ap()

    scw1 = nc.dram_tensor("scw1", [8 * 128 * DC, 512], BF16, kind="Internal").ap()
    scw2 = nc.dram_tensor("scw2", [DC * 128 * HC, 128], BF16, kind="Internal").ap()
    scqkv = nc.dram_tensor("scqkv", [24 * 128 * DC, 128], BF16, kind="Internal").ap()

    s_pe = SemC("s_pe")
    s_act = SemC("s_act")
    s_dve = SemC("s_dve")
    s_gpe = SemC("s_gpe")
    s_const = SemC("s_const")
    s_wload = [SemC(f"s_wl{i}") for i in range(4)]
    s_scwr = [SemC(f"s_scwr{i}") for i in range(4)]
    s_al = [SemC(f"s_al{i}") for i in range(4)]
    s_scq = [SemC(f"s_scq{i}") for i in range(4)]
    s_fw1 = [SemC("s_fw1a"), SemC("s_fw1b")]
    s_fw2 = [SemC("s_fw2a"), SemC("s_fw2b")]
    s_fqk = [SemC("s_fqka"), SemC("s_fqkb")]
    s_fvr = [SemC("s_fvra"), SemC("s_fvrb")]
    s_x = [SemC("s_x0"), SemC("s_x1")]
    s_out = SemC("s_out")
    sems = ([s_pe, s_act, s_dve, s_gpe, s_const, s_out]
            + s_wload + s_scwr + s_al + s_scq
            + s_fw1 + s_fw2 + s_fqk + s_fvr + s_x)
    rec = Rec(eng_sems={"scalar": s_act, "vector": s_dve, "gpsimd": s_gpe})

    with ExitStack() as ctx:
        sb = lambda n, s, d: ctx.enter_context(nc.sbuf_tensor(n, s, d))
        psa = lambda n, s, d: ctx.enter_context(nc.psum_tensor(n, s, d))

        xb = [sb("xb0", [128, DC * XE], F32), sb("xb1", [128, DC * XE], F32)]
        xn = sb("xn", [128, DC * MAXE], BF16)
        xcrot = sb("xcrot", [128, 4 * MAXE], BF16)
        sqrot = sb("sqrot", [128, 4 * MAXE], BF16)
        scr = [sb(f"scr{i}", [128, MAXE], F32) for i in range(3)]
        r3row = scr[2][0:1, 0:MAXE]
        rows_t = sb("rows_t", [1, 2 * MAXE], F32)
        rbf = sb("rbf", [1, 4 * MAXE], BF16)
        r1row = rows_t[0:1, 0:MAXE]
        r2row = rows_t[0:1, MAXE:2 * MAXE]
        g = sb("g", [128, HC * MAXE], BF16)
        qkrot = sb("qkrot", [128, 2 * MAXE], BF16)
        qphi = sb("qphi", [128, 4 * MAXE], BF16)
        kphi = sb("kphi", [128, 4 * MAXE], BF16)
        vtok = sb("vtok", [128, HEADS * 4 * VST], BF16)
        smA = sb("smA", [128, 2 * 128], BF16)
        smB = sb("smB", [6, 2 * 128], BF16)
        attn_n = sb("attn_n", [128, 2 * 128], BF16)
        attnT = sb("attnT", [128, HEADS * MAXE], BF16)
        zcol = sb("zcol", [128, 4], F32)
        w1s = sb("w1s", [128, 2 * 4096], BF16)
        w2s = sb("w2s", [128, 2 * 4096], BF16)
        qks = sb("qks", [128, 2 * 1024], BF16)
        vrs = sb("vrs", [128, 2 * 512], BF16)
        woutb = sb("woutb", [128, DC * 1024], BF16)
        projb = sb("projb", [128, FEAT], BF16)
        alpha1 = sb("alpha1", [128, HC], F32)
        alpha2 = sb("alpha2", [128, DC], F32)
        b1s = sb("b1s", [128, HC], F32)
        b2s = sb("b2s", [128, DC], F32)
        bouts = sb("bouts", [128, DC], F32)
        mA = sb("mA", [128, 3 * 128], F32)
        mB = sb("mB", [6, 128], F32)
        idb = sb("idb", [128, 128], BF16)
        onb = sb("onb", [128, 1], BF16)
        onb1 = sb("onb1", [1, 128], BF16)
        ccs = sb("ccs", [128, 3], F32)
        pf32 = sb("pf32", [128, 4 * 1024], F32)
        pbf = sb("pbf", [128, 4 * 1024], BF16)
        pa = sb("pa", [128, 3 * 1024], F32)
        patm = sb("patm", [128, 4], F32)

        PB = [psa(f"PB{i}", [128, 512], F32) for i in range(6)]
        PT = [psa("PT6", [128, 1024], BF16), psa("PT7", [128, 1024], BF16)]

        for s in sems:
            s.h = ctx.enter_context(nc.semaphore(s.name))

        bank_free = {}

        def bank_wait(bid):
            w = bank_free.get(bid)
            return [w] if w else []

        def t_mm(out, lhsT, rhs, start, stop, waits=(), inc=None):
            rec.op("tensor",
                   lambda E, o=out, l=lhsT, r=rhs, st=start, sp=stop:
                   E.matmul(o, l, r, start=st, stop=sp),
                   waits=waits, inc=inc)

        def barrier():
            snap = [(s_pe, s_pe.n), (s_act, s_act.n), (s_dve, s_dve.n)]
            own = {"tensor": s_pe, "scalar": s_act, "vector": s_dve}
            for e in ("tensor", "scalar", "vector"):
                rec.op(e, None, waits=[(s, v) for (s, v) in snap if s is not own[e]])

        def gdma(sem, dst, src, waits=()):
            v = sem.inc(16)
            rec.op("gpsimdq", lambda E, d=dst, s=src: E.dma_start(out=d, in_=s),
                   waits=waits, inc=(sem, 16))
            return v

        def sdma(sem, dst, src, waits=()):
            v = sem.inc(16)
            rec.op("sync", lambda E, d=dst, s=src: E.dma_start(out=d, in_=s),
                   waits=waits, inc=(sem, 16))
            return v

        offs = [sum(TILES[:i]) for i in range(len(TILES))]
        xload_v = {}
        out_v = {}

        def load_x(i):
            buf = xb[i % 2]
            E = TILES[i] + 6
            waits = [] if i < 2 else [(s_out, out_v[i - 2])]
            sem = s_x[i % 2]
            for c in range(DC):
                gdma(sem, buf[:, c * XE: c * XE + E],
                     xT[c * 128:(c + 1) * 128, offs[i]: offs[i] + E],
                     waits=waits if c == 0 else ())
            xload_v[i] = (sem, sem.n)

        load_x(0)
        if len(TILES) > 1:
            load_x(1)

        gdma(s_const, mA[:], maskA3)
        gdma(s_const, mB[:], maskB)
        gdma(s_const, idb[:], identb)
        gdma(s_const, onb[:], onesb)
        gdma(s_const, onb1[:], onesr)
        gdma(s_const, ccs[:], cconst)
        gdma(s_const, b1s[:], b1c)
        gdma(s_const, b2s[:], b2c)
        gdma(s_const, bouts[:], boutc)
        gdma(s_const, pa[:, 0:FEAT], proj)
        consts_v = s_const.n
        rec.op("gpsimd", lambda E: E.tensor_copy(projb[:], pa[:, 0:FEAT]),
               waits=[(s_const, consts_v)], inc=cinc(s_gpe))
        projb_v = s_gpe.n
        rec.op("gpsimd", lambda E: E.tensor_scalar_mul(b2s[:], b2s[:], 0.5),
               waits=[(s_const, consts_v)], inc=cinc(s_gpe))
        for h in range(HEADS):
            for tc in range(4):
                col = (h * 4 + tc) * VST + 128
                rec.op("gpsimd", lambda E, c=col: E.memset(vtok[:, c:c + 1], 1.0),
                       inc=cinc(s_gpe))
        v_ones = s_gpe.n

        scw1_g = scw1.rearrange("(g p k) c -> g p k c", p=128, k=DC)
        scw2_m = scw2.rearrange("(m p k) c -> m p k c", p=128, k=HC)
        scq_o = scqkv.rearrange("(o p k) c -> o p k c", p=128, k=DC)

        SLW = 1024
        slot_gate = [None] * 4
        wrf = [0] * 4

        def conv_step(it, load_src, consume_fn):
            sl = it % 4
            w = [slot_gate[sl]] if slot_gate[sl] else []
            v_l = sdma(s_wload[sl], pf32[:, sl * SLW:(sl + 1) * SLW], load_src,
                       waits=w)
            slot_gate[sl] = consume_fn(sl, [(s_wload[sl], v_l)])

        def sign_and_write(write_dst_fn):
            def consume(sl, waits):
                if wrf[sl]:
                    waits = waits + [(s_scwr[sl], wrf[sl])]
                rec.op("scalar",
                       lambda E, sl=sl: E.sign(pbf[:, sl * SLW:(sl + 1) * SLW],
                                               pf32[:, sl * SLW:(sl + 1) * SLW]),
                       waits=waits, inc=cinc(s_act))
                sv = s_act.n
                rec.op("scalarq", write_dst_fn(sl), waits=[(s_act, sv)],
                       inc=(s_scwr[sl], 16))
                wrf[sl] = s_scwr[sl].inc(16)
                return (s_act, sv)
            return consume

        w1grp_wait = {}
        for piece in range(4):
            for kc in range(DC):
                def _wr(sl, piece=piece, kc=kc):
                    return lambda E, sl=sl: E.dma_start(
                        out=scw1_g[piece * 2: piece * 2 + 2, :, kc, :].rearrange(
                            "g p c -> p g c"),
                        in_=pbf[:, sl * SLW:(sl + 1) * SLW].rearrange(
                            "p (g c) -> p g c", c=512))
                conv_step(piece * DC + kc,
                          w1t[kc * 128:(kc + 1) * 128,
                              piece * 1024:(piece + 1) * 1024],
                          sign_and_write(_wr))
            snap = [(s, s.n) for s in s_scwr]
            w1grp_wait[piece * 2] = snap
            w1grp_wait[piece * 2 + 1] = snap
        w1_scr_wait = w1grp_wait[7]

        a1col_v = [0] * HC
        a2col_v = [0] * DC
        deferred = {}
        aslot_gate = [None] * 3

        def alpha_step(it, load_src, reduce_fn):
            sl = it % 3
            w = [aslot_gate[sl]] if aslot_gate[sl] else []
            v_l = gdma(s_al[sl], pa[:, sl * SLW:(sl + 1) * SLW], load_src, waits=w)
            aslot_gate[sl] = reduce_fn(sl, [(s_al[sl], v_l)])

        def emit_deferred_prep():
            for hcn in range(HC):
                def _red(sl, waits, c=hcn):
                    rec.op("vector",
                           lambda E, sl=sl, c=c: E.tensor_reduce(
                               alpha1[:, c:c + 1], pa[:, sl * SLW:(sl + 1) * SLW],
                               axis=AXX, op=ALU.add, apply_absolute_value=True),
                           waits=waits, inc=cinc(s_dve))
                    rec.op("vector",
                           lambda E, c=c: E.tensor_scalar_mul(
                               alpha1[:, c:c + 1], alpha1[:, c:c + 1], 1.0 / D),
                           inc=cinc(s_dve))
                    a1col_v[c] = s_dve.n
                    return (s_dve, s_dve.n)
                alpha_step(hcn, w1n[hcn * 128:(hcn + 1) * 128, :], _red)

            for kc2 in range(HC):
                def _wr(sl, kc2=kc2):
                    return lambda E, sl=sl: E.dma_start(
                        out=scw2_m[:, :, kc2, :].rearrange("m p c -> p m c"),
                        in_=pbf[:, sl * SLW:(sl + 1) * SLW].rearrange(
                            "p (m c) -> p m c", c=128))
                conv_step(32 + kc2, w2t[kc2 * 128:(kc2 + 1) * 128, :],
                          sign_and_write(_wr))
            w2_scr_wait = [(s, s.n) for s in s_scwr]

            a2col_v.clear(); a2col_v.extend([0] * DC)
            for dcn in range(DC):
                for piece in range(4):
                    def _red(sl, waits, p=piece):
                        rec.op("vector",
                               lambda E, sl=sl, p=p: E.tensor_reduce(
                                   patm[:, p:p + 1], pa[:, sl * SLW:(sl + 1) * SLW],
                                   axis=AXX, op=ALU.add, apply_absolute_value=True),
                               waits=waits, inc=cinc(s_dve))
                        return (s_dve, s_dve.n)
                    alpha_step(32 + dcn * 4 + piece,
                               w2n[dcn * 128:(dcn + 1) * 128,
                                   piece * 1024:(piece + 1) * 1024], _red)
                rec.op("vector",
                       lambda E, c=dcn: E.tensor_reduce(
                           alpha2[:, c:c + 1], patm[:], axis=AXX, op=ALU.add),
                       inc=cinc(s_dve))
                rec.op("vector",
                       lambda E, c=dcn: E.tensor_scalar_mul(
                           alpha2[:, c:c + 1], alpha2[:, c:c + 1], 0.5 / H),
                       inc=cinc(s_dve))
                a2col_v[dcn] = s_dve.n

            wrq = [0] * 4
            def cast_and_write(write_dst_fn):
                def consume(sl, waits):
                    if wrf[sl]:
                        waits = waits + [(s_scwr[sl], wrf[sl])]
                    if wrq[sl]:
                        waits = waits + [(s_scq[sl], wrq[sl])]
                    rec.op("vector",
                           lambda E, sl=sl: E.tensor_copy(
                               pbf[:, sl * SLW:(sl + 1) * SLW],
                               pf32[:, sl * SLW:(sl + 1) * SLW]),
                           waits=waits, inc=cinc(s_dve))
                    cv = s_dve.n
                    rec.op("gpsimdq", write_dst_fn(sl), waits=[(s_dve, cv)],
                           inc=(s_scq[sl], 16))
                    wrq[sl] = s_scq[sl].inc(16)
                    return (s_dve, cv)
                return consume

            for kc in range(DC):
                for hh3 in range(3):
                    def _wr(sl, kc=kc, hh3=hh3):
                        return lambda E, sl=sl: E.dma_start(
                            out=scq_o[hh3 * 8:(hh3 + 1) * 8, :, kc, :].rearrange(
                                "o p c -> p o c"),
                            in_=pbf[:, sl * SLW:(sl + 1) * SLW].rearrange(
                                "p (o c) -> p o c", c=128))
                    conv_step(64 + kc * 3 + hh3,
                              wqkv[kc * 128:(kc + 1) * 128,
                                   hh3 * 1024:(hh3 + 1) * 1024],
                              cast_and_write(_wr))
            qkv_scr_wait = [(s, s.n) for s in s_scq]

            wout_done = 0
            for kc in range(DC):
                def _cast(sl, waits, c=kc):
                    rec.op("vector",
                           lambda E, sl=sl, c=c: E.tensor_copy(
                               woutb[:, c * 1024:(c + 1) * 1024],
                               pf32[:, sl * SLW:(sl + 1) * SLW]),
                           waits=waits, inc=cinc(s_dve))
                    return (s_dve, s_dve.n)
                conv_step(88 + kc, wout[kc * 128:(kc + 1) * 128, :], _cast)
            wout_done = s_dve.n
            deferred['wout_done'] = wout_done
            deferred['w2_scr'] = w2_scr_wait
            deferred['qkv_scr'] = qkv_scr_wait

        bc_state = {"v": 0}
        ln_state = {"xc": [None] * 4, "sq": [0] * 4, "xn_rd": 0}

        def ln_emit(buf, off, W):
            bsum, bsq, bcA, bcB = PB[0], PB[1], PB[2], PB[3]
            rbf_war = [(s_pe, bc_state["v"])] if bc_state["v"] else []
            xc_free = ln_state["xc"]
            sq_free = ln_state["sq"]
            for c in range(DC):
                sl = c % 4
                w = list(xc_free[sl]) if xc_free[sl] else []
                rec.op("vector",
                       lambda E, c=c, sl=sl: E.tensor_copy(
                           xcrot[:, sl * MAXE: sl * MAXE + W],
                           buf[:, c * XE + off: c * XE + off + W]),
                       waits=w, inc=cinc(s_dve))
                cast_v = s_dve.n
                w = [(s_dve, cast_v)]
                if sq_free[sl]:
                    w.append((s_pe, sq_free[sl]))
                rec.op("scalar",
                       lambda E, sl=sl: E.square(sqrot[:, sl * MAXE: sl * MAXE + W],
                                                 xcrot[:, sl * MAXE: sl * MAXE + W]),
                       waits=w, inc=cinc(s_act))
                sq_v = s_act.n
                t_mm(bsum[0:1, 0:W], onb[:, 0:1], xcrot[:, sl * MAXE: sl * MAXE + W],
                     start=(c == 0), stop=(c == DC - 1),
                     waits=[(s_dve, cast_v)] + (bank_wait(0) if c == 0 else []),
                     inc=cinc(s_pe))
                xc_free[sl] = [(s_pe, s_pe.n), (s_act, sq_v)]
                t_mm(bsq[0:1, 0:W], onb[:, 0:1], sqrot[:, sl * MAXE: sl * MAXE + W],
                     start=(c == 0), stop=(c == DC - 1),
                     waits=[(s_act, sq_v)] + (bank_wait(1) if c == 0 else []),
                     inc=cinc(s_pe))
                sq_free[sl] = s_pe.n
            stats_mm = s_pe.n
            rec.op("scalar",
                   lambda E: E.activation(r1row[:, 0:W], bsum[0:1, 0:W],
                                          AF.Square, scale=ccs[0:1, 2:3]),
                   waits=[(s_pe, stats_mm)], inc=cinc(s_act))
            m2_v = s_act.n
            rec.op("vector",
                   lambda E: E.scalar_tensor_tensor(r2row[:, 0:W], bsq[0:1, 0:W],
                                                    1.0 / D, r1row[:, 0:W],
                                                    ALU.mult, ALU.subtract),
                   waits=[(s_pe, stats_mm), (s_act, m2_v)], inc=cinc(s_dve))
            var_v = s_dve.n
            rec.op("scalar",
                   lambda E: E.activation(r1row[:, 0:W], r2row[:, 0:W],
                                          AF.Ln, bias=ccs[0:1, 0:1]),
                   waits=[(s_dve, var_v)], inc=cinc(s_act))
            rec.op("scalar",
                   lambda E: E.activation(r2row[:, 0:W], r1row[:, 0:W],
                                          AF.Exp, scale=ccs[0:1, 1:2]),
                   inc=cinc(s_act))
            rstd_v = s_act.n
            rec.op("vector",
                   lambda E: E.scalar_tensor_tensor(r1row[:, 0:W], bsum[0:1, 0:W],
                                                    1.0 / D, r2row[:, 0:W],
                                                    ALU.mult, ALU.mult),
                   waits=[(s_act, rstd_v)], inc=cinc(s_dve))
            mr_v = s_dve.n
            rec.op("vector",
                   lambda E: E.tensor_copy(rbf[0:1, 0:W], r2row[:, 0:W]),
                   waits=[(s_act, rstd_v)] + rbf_war, inc=cinc(s_dve))
            rec.op("vector",
                   lambda E: E.tensor_sub(r3row[:, 0:W], r2row[:, 0:W],
                                          rbf[0:1, 0:W]),
                   inc=cinc(s_dve))
            rec.op("vector",
                   lambda E: E.tensor_copy(rbf[0:1, MAXE:MAXE + W], r3row[:, 0:W]),
                   inc=cinc(s_dve))
            gA = s_dve.n
            rec.op("vector",
                   lambda E: E.tensor_copy(rbf[0:1, 2 * MAXE:2 * MAXE + W],
                                           r1row[:, 0:W]),
                   inc=cinc(s_dve))
            rec.op("vector",
                   lambda E: E.tensor_sub(r3row[:, 0:W], r1row[:, 0:W],
                                          rbf[0:1, 2 * MAXE:2 * MAXE + W]),
                   inc=cinc(s_dve))
            rec.op("vector",
                   lambda E: E.tensor_copy(rbf[0:1, 3 * MAXE:3 * MAXE + W],
                                           r3row[:, 0:W]),
                   inc=cinc(s_dve))
            gB = s_dve.n
            t_mm(bcA[:, 0:W], onb1[:], rbf[0:1, 0:W], True, False,
                 waits=[(s_dve, gA)] + bank_wait(2))
            t_mm(bcA[:, 0:W], onb1[:], rbf[0:1, MAXE:MAXE + W], False, True,
                 inc=cinc(s_pe))
            t_mm(bcB[:, 0:W], onb1[:], rbf[0:1, 2 * MAXE:2 * MAXE + W], True, False,
                 waits=[(s_dve, gB)] + bank_wait(3))
            t_mm(bcB[:, 0:W], onb1[:], rbf[0:1, 3 * MAXE:3 * MAXE + W], False, True,
                 inc=cinc(s_pe))
            bc_state["v"] = s_pe.n
            bc_v = s_pe.n
            xn_vc = [0] * DC
            for c in range(DC):
                sl = c % 2
                w = [(s_pe, bc_v)] if c == 0 else []
                rec.op("vector",
                       lambda E, c=c, sl=sl: E.tensor_tensor(
                           scr[sl][:, 0:W], buf[:, c * XE + off: c * XE + off + W],
                           bcA[:, 0:W], ALU.mult),
                       waits=w, inc=cinc(s_dve))
                w2_ = [(s_pe, ln_state["xn_rd"])] if (c == 0 and ln_state["xn_rd"]) else []
                rec.op("vector",
                       lambda E, c=c, sl=sl: E.tensor_tensor(
                           xn[:, c * MAXE: c * MAXE + W], scr[sl][:, 0:W],
                           bcB[:, 0:W], ALU.subtract),
                       waits=w2_, inc=cinc(s_dve))
                xn_vc[c] = s_dve.n
            xn_v = s_dve.n
            bank_free[0] = (s_dve, mr_v)
            bank_free[1] = (s_dve, var_v)
            bank_free[2] = (s_dve, xn_v)
            bank_free[3] = (s_dve, xn_v)
            return xn_vc

        w1f = {"v": {0: 0, 1: 0}, "freed": {0: 0, 1: 0}}
        w2f = {"v": {0: 0, 1: 0}, "freed": {0: 0, 1: 0}}
        qkf = {"v": {0: 0, 1: 0}, "freed": {0: 0, 1: 0}}

        def ffn_emit(buf, off, W, first_tile, xn_vc):
            evac_last = 0
            for hc in range(HC):
                gidx, slot = hc // 4, (hc // 4) % 2
                if hc % 4 == 0:
                    w = list(w1grp_wait[gidx]) if first_tile else []
                    if w1f["freed"][slot]:
                        w.append((s_pe, w1f["freed"][slot]))
                    w1f["v"][slot] = sdma(
                        s_fw1[slot],
                        w1s[:, slot * 4096:(slot + 1) * 4096].rearrange(
                            "p (k c) -> p k c", c=512),
                        scw1_g[gidx], waits=w)
                bid = 2 + (hc % 4)
                for kc in range(DC):
                    waits = ([(s_fw1[slot], w1f["v"][slot])] + bank_wait(bid)) if kc == 0 else []
                    if hc == 0:
                        waits = waits + [(s_dve, xn_vc[kc])]
                    t_mm(PB[bid][:, 0:W],
                         w1s[:, slot * 4096 + kc * 512 + (hc % 4) * 128:
                             slot * 4096 + kc * 512 + (hc % 4) * 128 + 128],
                         xn[:, kc * MAXE: kc * MAXE + W],
                         start=(kc == 0), stop=(kc == DC - 1),
                         waits=waits, inc=cinc(s_pe) if kc == DC - 1 else None)
                mm_v = s_pe.n
                if hc % 4 == 3:
                    w1f["freed"][slot] = mm_v
                wg = [(s_dve, a1col_v[hc]), (s_pe, mm_v)]
                rec.op("scalar",
                       lambda E, hc=hc, bid=bid: E.activation(
                           g[:, hc * MAXE: hc * MAXE + W], PB[bid][:, 0:W], AF.Gelu,
                           scale=alpha1[:, hc:hc + 1], bias=b1s[:, hc:hc + 1]),
                       waits=wg,
                       inc=cinc(s_act))
                evac_last = s_act.n
                bank_free[bid] = (s_act, evac_last)
            ln_state["xn_rd"] = s_pe.n
            resid_free = {0: 0, 1: 0}
            for mc in range(DC):
                slot = mc % 2
                w = list(deferred['w2_scr']) if first_tile else []
                if w2f["freed"][slot]:
                    w.append((s_pe, w2f["freed"][slot]))
                w2f["v"][slot] = sdma(
                    s_fw2[slot],
                    w2s[:, slot * 4096:(slot + 1) * 4096].rearrange(
                        "p (k c) -> p k c", c=128),
                    scw2_m[mc], waits=w)
                bid = 2 + (mc % 4)
                bw = bank_wait(bid)
                for kc2 in range(HC):
                    waits = ([(s_fw2[slot], w2f["v"][slot]), (s_act, evac_last)] + bw) \
                        if kc2 == 0 else []
                    t_mm(PB[bid][:, 0:W],
                         w2s[:, slot * 4096 + kc2 * 128: slot * 4096 + kc2 * 128 + 128],
                         g[:, kc2 * MAXE: kc2 * MAXE + W],
                         start=(kc2 == 0), stop=(kc2 == HC - 1),
                         waits=waits, inc=cinc(s_pe) if kc2 == HC - 1 else None)
                mm_v = s_pe.n
                w2f["freed"][slot] = mm_v
                sl = mc % 2
                w = [(s_dve, a2col_v[mc]), (s_pe, mm_v)]
                if resid_free[sl]:
                    w.append((s_dve, resid_free[sl]))
                rec.op("scalar",
                       lambda E, bid=bid, sl=sl, mc=mc: E.activation(
                           scr[sl][:, 0:W], PB[bid][:, 0:W], AF.Identity,
                           scale=alpha2[:, mc:mc + 1], bias=b2s[:, mc:mc + 1]),
                       waits=w, inc=cinc(s_act))
                ev = s_act.n
                rec.op("vector",
                       lambda E, mc=mc, sl=sl: E.tensor_add(
                           buf[:, mc * XE + off: mc * XE + off + W],
                           buf[:, mc * XE + off: mc * XE + off + W],
                           scr[sl][:, 0:W]),
                       waits=[(s_act, ev)], inc=cinc(s_dve))
                resid_free[sl] = s_dve.n
                bank_free[bid] = (s_act, ev)

        def qkv_emit(E, first_tile, xn_vc):
            scrw = list(deferred['qkv_scr']) if first_tile else []
            qkrot_free = {0: 0, 1: 0}
            phi_free = {0: 0, 1: 0}
            scr_free = {0: 0, 1: 0}
            for oc in range(16):
                slot = oc % 2
                w = list(scrw)
                if qkf["freed"][slot]:
                    w.append((s_pe, qkf["freed"][slot]))
                qkf["v"][slot] = sdma(
                    s_fqk[slot],
                    qks[:, slot * 1024:(slot + 1) * 1024].rearrange(
                        "p (k c) -> p k c", c=128),
                    scq_o[oc], waits=w)
                bid = 2 + slot
                bw = bank_wait(bid)
                for kc in range(DC):
                    waits = ([(s_fqk[slot], qkf["v"][slot])] + bw) if kc == 0 else []
                    if oc == 0:
                        waits = waits + [(s_dve, xn_vc[kc])]
                    t_mm(PB[bid][:, 0:E],
                         qks[:, slot * 1024 + kc * 128: slot * 1024 + kc * 128 + 128],
                         xn[:, kc * MAXE: kc * MAXE + E],
                         start=(kc == 0), stop=(kc == DC - 1),
                         waits=waits, inc=cinc(s_pe) if kc == DC - 1 else None)
                mm_v = s_pe.n
                qkf["freed"][slot] = mm_v
                w = [(s_pe, mm_v)]
                if qkrot_free[slot]:
                    w.append((s_pe, qkrot_free[slot]))
                rec.op("scalar",
                       lambda E_, sl=slot, bid=bid: E_.copy(
                           qkrot[:, sl * MAXE: sl * MAXE + E], PB[bid][:, 0:E]),
                       waits=w, inc=cinc(s_act))
                evq = s_act.n
                bank_free[bid] = (s_act, evq)
                pbid = 4 + slot
                head = oc % 8
                w = [(s_act, evq)]
                if phi_free[slot]:
                    w.append((s_dve, phi_free[slot]))
                else:
                    w += bank_wait(pbid)
                w.append((s_gpe, projb_v))
                t_mm(PB[pbid][0:FEAT, 0:E], projb[:],
                     qkrot[:, slot * MAXE: slot * MAXE + E],
                     True, True, waits=w, inc=cinc(s_pe))
                phi_v = s_pe.n
                qkrot_free[slot] = phi_v
                rows = (head % 2) * 64
                w = [(s_pe, phi_v)]
                if scr_free[slot]:
                    w.append((s_dve, scr_free[slot]))
                rec.op("scalar",
                       lambda E_, pbid=pbid, rows=rows: E_.activation(
                           scr[0][rows:rows + 64, 0:E], PB[pbid][0:FEAT, 0:E], AF.Relu),
                       waits=w, inc=cinc(s_act))
                r_v = s_act.n
                rec.op("vector",
                       lambda E_, pbid=pbid, rows=rows: E_.tensor_sub(
                           scr[1][rows:rows + 64, 0:E], PB[pbid][0:FEAT, 0:E],
                           scr[0][rows:rows + 64, 0:E]),
                       waits=[(s_act, r_v)], inc=cinc(s_dve))
                m_v = s_dve.n
                phi_free[slot] = m_v
                rec.op("scalar",
                       lambda E_, rows=rows: E_.activation(
                           scr[2][rows:rows + 64, 0:E], scr[1][rows:rows + 64, 0:E],
                           AF.Exp),
                       waits=[(s_dve, m_v)], inc=cinc(s_act))
                e_v = s_act.n
                dst = qphi if oc < 8 else kphi
                pair = head // 2
                rec.op("vector",
                       lambda E_, dst=dst, pair=pair, rows=rows: E_.tensor_add(
                           dst[rows:rows + 64, pair * MAXE: pair * MAXE + E],
                           scr[2][rows:rows + 64, 0:E], scr[0][rows:rows + 64, 0:E]),
                       waits=[(s_act, e_v)], inc=cinc(s_dve))
                scr_free[slot] = s_dve.n
            bank_free[4] = (s_dve, phi_free[0])
            bank_free[5] = (s_dve, phi_free[1])
            nch = (E + 127) // 128
            vr_free = {0: 0, 1: 0}
            for h2 in range(2):
                vmm_v = 0
                for kc in range(DC):
                    slot = kc % 2
                    w = list(scrw)
                    if vr_free[slot]:
                        w.append((s_pe, vr_free[slot]))
                    v_l = sdma(s_fvr[slot],
                               vrs[:, slot * 512:(slot + 1) * 512].rearrange(
                                   "p (o c) -> p o c", c=128),
                               scq_o[16 + h2 * 4: 16 + h2 * 4 + 4, :, kc, :].rearrange(
                                   "o p c -> p o c"),
                               waits=w)
                    for tc in range(nch):
                        rows = min(128, E - tc * 128)
                        bid = 2 + tc
                        waits = [(s_fvr[slot], v_l)] + (bank_wait(bid) if kc == 0 else [])
                        t_mm(PB[bid][0:rows, 0:512],
                             xn[:, kc * MAXE + tc * 128: kc * MAXE + tc * 128 + rows],
                             vrs[:, slot * 512:(slot + 1) * 512],
                             start=(kc == 0), stop=(kc == DC - 1),
                             waits=waits,
                             inc=cinc(s_pe) if tc == nch - 1 else None)
                    vr_free[slot] = s_pe.n
                vmm_v = s_pe.n
                for tc in range(nch):
                    rows = min(128, E - tc * 128)
                    bid = 2 + tc
                    for hh in range(4):
                        head = h2 * 4 + hh
                        col = (head * 4 + tc) * VST
                        rec.op("scalar",
                               lambda E_, bid=bid, rows=rows, hh=hh, col=col: E_.copy(
                                   vtok[0:rows, col:col + 128],
                                   PB[bid][0:rows, hh * 128: hh * 128 + 128]),
                               waits=[(s_pe, vmm_v)], inc=cinc(s_act))
                    bank_free[bid] = (s_act, s_act.n)
            ln_state["xn_rd"] = s_pe.n

        def attn_emit(T, E, first, last):
            nqb = (T + 127) // 128
            sm_free = {0: 0, 1: 0}
            sAB_free = {0: 0, 1: 0}
            attn_free = {0: 0, 1: 0}
            an_free = {0: 0, 1: 0}
            pt_free = {0: 0, 1: 0}
            idx = 0
            for m in range(nqb):
                qb = min(128, T - m * 128)
                arows = min(128, E - m * 128)
                brows = max(0, qb - 122)
                mvar = 0 if (first and m == 0) else (2 if (last and m == nqb - 1) else 1)
                for head in range(HEADS):
                    slot = idx % 2
                    pair, rows = head // 2, (head % 2) * 64
                    w = [(s_dve, sAB_free[slot])] if sAB_free[slot] else []
                    t_mm(PB[0 + slot][0:arows, 0:qb],
                         kphi[rows:rows + 64,
                              pair * MAXE + m * 128: pair * MAXE + m * 128 + arows],
                         qphi[rows:rows + 64,
                              pair * MAXE + 3 + m * 128: pair * MAXE + 3 + m * 128 + qb],
                         True, True, waits=w,
                         inc=cinc(s_pe) if brows == 0 else None)
                    if brows > 0:
                        t_mm(PB[2 + slot][0:brows, 0:qb],
                             kphi[rows:rows + 64,
                                  pair * MAXE + (m + 1) * 128:
                                  pair * MAXE + (m + 1) * 128 + brows],
                             qphi[rows:rows + 64,
                                  pair * MAXE + 3 + m * 128:
                                  pair * MAXE + 3 + m * 128 + qb],
                             True, True, waits=(), inc=cinc(s_pe))
                    sc_v = s_pe.n
                    w = [(s_pe, sc_v)]
                    if sm_free[slot]:
                        w.append((s_pe, sm_free[slot]))
                    rec.op("vector",
                           lambda E_, slot=slot, arows=arows, qb=qb, mvar=mvar:
                           E_.tensor_tensor(
                               smA[0:arows, slot * 128: slot * 128 + qb],
                               PB[0 + slot][0:arows, 0:qb],
                               mA[0:arows, mvar * 128: mvar * 128 + qb], ALU.mult),
                           waits=w,
                           inc=cinc(s_dve) if brows == 0 else None)
                    if brows > 0:
                        rec.op("vector",
                               lambda E_, slot=slot, brows=brows, qb=qb:
                               E_.tensor_tensor(
                                   smB[0:brows, slot * 128: slot * 128 + qb],
                                   PB[2 + slot][0:brows, 0:qb],
                                   mB[0:brows, 0:qb], ALU.mult),
                               inc=cinc(s_dve))
                    sm_v = s_dve.n
                    sAB_free[slot] = sm_v
                    abid = 4 + slot
                    w = [(s_dve, sm_v), (s_gpe, v_ones)]
                    if attn_free[slot]:
                        w.append((s_act, attn_free[slot]))
                    t_mm(PB[abid][0:qb, 0:129],
                         smA[0:arows, slot * 128: slot * 128 + qb],
                         vtok[0:arows, (head * 4 + m) * VST: (head * 4 + m) * VST + 129],
                         True, (brows == 0), waits=w,
                         inc=cinc(s_pe) if brows == 0 else None)
                    if brows > 0:
                        t_mm(PB[abid][0:qb, 0:129],
                             smB[0:brows, slot * 128: slot * 128 + qb],
                             vtok[0:brows,
                                  (head * 4 + m + 1) * VST: (head * 4 + m + 1) * VST + 129],
                             False, True, waits=(), inc=cinc(s_pe))
                    at_v = s_pe.n
                    sm_free[slot] = at_v
                    rec.op("vector",
                           lambda E_, abid=abid, qb=qb, slot=slot: E_.tensor_scalar(
                               zcol[0:qb, 2 * slot: 2 * slot + 1],
                               PB[abid][0:qb, 128:129], EPS_Z, None, ALU.add),
                           waits=[(s_pe, at_v)])
                    rec.op("vector",
                           lambda E_, qb=qb, slot=slot: E_.reciprocal(
                               zcol[0:qb, 2 * slot + 1: 2 * slot + 2],
                               zcol[0:qb, 2 * slot: 2 * slot + 1]),
                           inc=cinc(s_dve))
                    zr_v = s_dve.n
                    w = [(s_dve, zr_v)]
                    if an_free[slot]:
                        w.append((s_pe, an_free[slot]))
                    rec.op("scalar",
                           lambda E_, abid=abid, qb=qb, slot=slot: E_.activation(
                               attn_n[0:qb, slot * 128: slot * 128 + 128],
                               PB[abid][0:qb, 0:128], AF.Identity,
                               scale=zcol[0:qb, 2 * slot + 1: 2 * slot + 2]),
                           waits=w, inc=cinc(s_act))
                    an_v = s_act.n
                    attn_free[slot] = an_v
                    w = [(s_act, an_v)]
                    if pt_free[slot]:
                        w.append((s_act, pt_free[slot]))
                    rec.op("tensor",
                           lambda E_, slot=slot, qb=qb: E_.transpose(
                               PT[slot][0:128, 0:qb],
                               attn_n[0:qb, slot * 128: slot * 128 + 128],
                               idb[0:qb, 0:qb]),
                           waits=w, inc=cinc(s_pe))
                    tr_v = s_pe.n
                    an_free[slot] = tr_v
                    rec.op("scalar",
                           lambda E_, slot=slot, head=head, m=m, qb=qb: E_.copy(
                               attnT[:, head * MAXE + m * 128: head * MAXE + m * 128 + qb],
                               PT[slot][0:128, 0:qb]),
                           waits=[(s_pe, tr_v)], inc=cinc(s_act))
                    pt_free[slot] = s_act.n
                    idx += 1

        def outproj_emit(buf, T):
            resid_free = {0: 0, 1: 0}
            for mc in range(DC):
                bid = 2 + (mc % 4)
                bw = bank_wait(bid)
                for kc in range(DC):
                    waits = ([(s_dve, deferred['wout_done'])] + bw) if kc == 0 else []
                    t_mm(PB[bid][:, 0:T],
                         woutb[:, kc * 1024 + mc * 128: kc * 1024 + mc * 128 + 128],
                         attnT[:, kc * MAXE: kc * MAXE + T],
                         start=(kc == 0), stop=(kc == DC - 1),
                         waits=waits, inc=cinc(s_pe) if kc == DC - 1 else None)
                mm_v = s_pe.n
                sl = mc % 2
                w = [(s_pe, mm_v)]
                if resid_free[sl]:
                    w.append((s_dve, resid_free[sl]))
                rec.op("scalar",
                       lambda E_, bid=bid, sl=sl, mc=mc, T=T: E_.activation(
                           scr[sl][:, 0:T], PB[bid][:, 0:T], AF.Identity,
                           bias=bouts[:, mc:mc + 1]),
                       waits=w, inc=cinc(s_act))
                ev = s_act.n
                rec.op("vector",
                       lambda E_, mc=mc, sl=sl, T=T: E_.tensor_add(
                           buf[:, mc * XE + 3: mc * XE + 3 + T],
                           buf[:, mc * XE + 3: mc * XE + 3 + T], scr[sl][:, 0:T]),
                       waits=[(s_act, ev)], inc=cinc(s_dve))
                resid_free[sl] = s_dve.n
                bank_free[bid] = (s_act, ev)

        for i, T in enumerate(TILES):
            E = T + 6
            buf = xb[i % 2]
            first_tile = (i == 0)
            for e in ("tensor", "scalar", "vector"):
                rec.op(e, None, waits=[xload_v[i], (s_const, consts_v)])
            xn_vc = ln_emit(buf, 0, E)
            if i == 0:
                emit_deferred_prep()
            ffn_emit(buf, 0, E, first_tile, xn_vc)
            if stage >= 2:
                xn_vc = ln_emit(buf, 0, E)
                qkv_emit(E, first_tile, xn_vc)
                barrier()
                attn_emit(T, E, first=(i == 0), last=(i == len(TILES) - 1))
                barrier()
                outproj_emit(buf, T)
            if stage >= 3:
                xn_vc = ln_emit(buf, 3, T)
                ffn_emit(buf, 3, T, False, xn_vc)
            store_wait = [(s_dve, s_dve.n)]
            for c in range(DC):
                gdma(s_out, yT[c * 128:(c + 1) * 128, offs[i]: offs[i] + T],
                     buf[:, c * XE + 3: c * XE + 3 + T],
                     waits=store_wait if c == 0 else ())
            out_v[i] = s_out.n
            if i + 2 < len(TILES):
                load_x(i + 2)

        rec.op("gpsimd", None, waits=[(s_out, out_v[len(TILES) - 1])])

        block = ctx.enter_context(nc.Block())

        def emit_for(name):
            extra = {"gpsimd": {"gpsimdq"}, "scalar": {"scalarq"}}
            names = {name} | extra.get(name, set())

            def emitter(engobj):
                for eng, fn, waits, incs in rec.ops:
                    if eng not in names:
                        continue
                    for s, v in waits:
                        engobj.wait_ge(s.h, v)
                    if fn is None:
                        if incs:
                            inst = engobj.nop()
                            for s, amt in incs:
                                inst.then_inc(s.h, amt)
                        continue
                    inst = fn(engobj)
                    for s, amt in incs:
                        inst.then_inc(s.h, amt)
            return emitter

        block.tensor(emit_for("tensor"))
        block.scalar(emit_for("scalar"))
        block.vector(emit_for("vector"))
        block.sync(emit_for("sync"))
        block.gpsimd(emit_for("gpsimd"))

    return nc


def _host_inputs(L0_attn_x, L0_attn_w_qkv, L0_attn_proj, L0_attn_w_out, L0_attn_b_out,
                 L0_ffn_w1, L0_ffn_b1, L0_ffn_w2, L0_ffn_b2):
    x = np.asarray(L0_attn_x, dtype=np.float32)
    wqkv = np.ascontiguousarray(np.asarray(L0_attn_w_qkv, np.float32))
    projm = np.ascontiguousarray(np.asarray(L0_attn_proj, np.float32))
    woutm = np.ascontiguousarray(np.asarray(L0_attn_w_out, np.float32))
    bout = np.asarray(L0_attn_b_out, np.float32)
    w1 = np.asarray(L0_ffn_w1, np.float32)
    b1 = np.asarray(L0_ffn_b1, np.float32)
    w2 = np.asarray(L0_ffn_w2, np.float32)
    b2 = np.asarray(L0_ffn_b2, np.float32)

    w1t = np.ascontiguousarray(w1.T)
    w2t = np.ascontiguousarray(w2.T)
    b1cm = np.ascontiguousarray(b1.reshape(HC, 128).T)
    b2cm = np.ascontiguousarray(b2.reshape(DC, 128).T)
    boutcm = np.ascontiguousarray(bout.reshape(DC, 128).T)
    identb = np.eye(128, dtype=ml_dtypes.bfloat16)
    onesb = np.ones((128, 1), dtype=ml_dtypes.bfloat16)
    onesr = np.ones((1, 128), dtype=ml_dtypes.bfloat16)
    ones1 = np.ones((1, 128), dtype=np.float32)
    cconst = np.broadcast_to(np.array([EPS_LN, -0.5, 1.0 / D], np.float32),
                             (128, 3)).copy()

    j = np.arange(128)[:, None]
    q = np.arange(128)[None, :]
    bandA = ((j - q >= 0) & (j - q <= 6)).astype(np.float32)
    jb = np.arange(6)[:, None]
    bandB = (q >= 122 + jb).astype(np.float32)
    T_last = TILES[-1]
    cut_last = (T_last + 3) - 384

    in_maps = []
    for core in range(8):
        b = core // 4
        start = (core % 4) * NTOK
        lo, hi = start - 3, start + NTOK + 3
        xs = np.zeros((NTOK + 6, D), np.float32)
        clo, chi = max(lo, 0), min(hi, 16384)
        xs[clo - lo: chi - lo, :] = x[b, clo:chi, :]
        xTm = np.ascontiguousarray(xs.T)

        Afirst = bandA.copy()
        Alast = bandA.copy()
        if start == 0:
            Afirst[0:3, :] = 0.0
        if start + NTOK == 16384:
            Alast[cut_last:cut_last + 3, :] = 0.0
        maskA3m = np.ascontiguousarray(np.concatenate([Afirst, bandA, Alast], axis=1))

        in_maps.append({
            "xT": xTm, "wqkv": wqkv, "wout": woutm, "proj": projm,
            "w1t": w1t, "w1n": w1, "w2t": w2t, "w2n": w2,
            "b1c": b1cm, "b2c": b2cm, "boutc": boutcm,
            "maskA3": maskA3m, "maskB": np.ascontiguousarray(bandB),
            "identb": identb, "onesb": onesb, "onesr": onesr, "ones1": ones1, "cconst": cconst,
        })
    return in_maps


def kernel(L0_attn_x, L0_attn_w_qkv, L0_attn_proj, L0_attn_w_out, L0_attn_b_out,
           L0_ffn_w1, L0_ffn_b1, L0_ffn_w2, L0_ffn_b2):
    in_maps = _host_inputs(L0_attn_x, L0_attn_w_qkv, L0_attn_proj, L0_attn_w_out,
                           L0_attn_b_out, L0_ffn_w1, L0_ffn_b1, L0_ffn_w2, L0_ffn_b2)
    if "nc" not in _CACHED:
        _CACHED["nc"] = build_nc(STAGE)
    nc = _CACHED["nc"]
    res = run_bass_kernel_spmd(nc, in_maps, core_ids=list(range(8)))
    out = np.empty((2, 16384, D), np.float32)
    for core in range(8):
        b = core // 4
        start = (core % 4) * NTOK
        out[b, start:start + NTOK, :] = res.results[core]["yT"].T
    return out


# version 10
# speedup vs baseline: 1.0082x; 1.0082x over previous
import numpy as np
import ml_dtypes
from contextlib import ExitStack

import concourse.bass as bass
import concourse.mybir as mybir
from concourse.bass_utils import run_bass_kernel_spmd

F32 = mybir.dt.float32
BF16 = mybir.dt.bfloat16
AF = mybir.ActivationFunctionType
ALU = mybir.AluOpType
AXX = mybir.AxisListType.X

D = 1024
DC = 8
H = 4096
HC = 32
HEADS = 8
DH = 128
FEAT = 64
NTOK = 4096
TILES = [456] * 8 + [448]
MAXE = 462
XE = 512
VST = 130
assert sum(TILES) == NTOK
EPS_LN = 1e-5
EPS_Z = 1e-8
STAGE = 3

_CACHED = {}


class SemC:
    def __init__(self, name):
        self.name = name
        self.h = None
        self.n = 0

    def inc(self, amt=1):
        self.n += amt
        return self.n


class Rec:
    def __init__(self, eng_sems=None):
        self.ops = []
        self.eng_sems = eng_sems or {}

    def op(self, eng, fn=None, waits=(), inc=None):
        incs = ()
        if inc is not None:
            incs = tuple(inc) if isinstance(inc, list) else (inc,)
        waits = list(waits)
        own = self.eng_sems.get(eng)
        if own is not None and fn is not None:
            own_amt = sum(a for (s, a) in incs if s is own)
            if own_amt == 0:
                own.inc(1)
                incs = incs + ((own, 1),)
                own_amt = 1
            waits.append((own, own.n - own_amt))
        waits = tuple((s, v) for (s, v) in waits if v > 0)
        for s, v in waits:
            assert v <= s.n, f"wait-on-future: eng={eng} sem={s.name} v={v} planned={s.n}"
        self.ops.append((eng, fn, waits, incs))


def cinc(sem):
    sem.inc(1)
    return (sem, 1)


def build_nc(stage):
    nc = bass.Bass("TRN2", target_bir_lowering=False, debug=False)

    xT = nc.dram_tensor("xT", [D, NTOK + 6], F32, kind="ExternalInput").ap()
    wqkv = nc.dram_tensor("wqkv", [D, 3 * D], F32, kind="ExternalInput").ap()
    wout = nc.dram_tensor("wout", [D, D], F32, kind="ExternalInput").ap()
    proj = nc.dram_tensor("proj", [DH, FEAT], F32, kind="ExternalInput").ap()
    w1t = nc.dram_tensor("w1t", [D, H], F32, kind="ExternalInput").ap()
    w1n = nc.dram_tensor("w1n", [H, D], F32, kind="ExternalInput").ap()
    w2t = nc.dram_tensor("w2t", [H, D], F32, kind="ExternalInput").ap()
    w2n = nc.dram_tensor("w2n", [D, H], F32, kind="ExternalInput").ap()
    b1c = nc.dram_tensor("b1c", [128, HC], F32, kind="ExternalInput").ap()
    b2c = nc.dram_tensor("b2c", [128, DC], F32, kind="ExternalInput").ap()
    boutc = nc.dram_tensor("boutc", [128, DC], F32, kind="ExternalInput").ap()
    maskA3 = nc.dram_tensor("maskA3", [128, 3 * 128], F32, kind="ExternalInput").ap()
    maskB = nc.dram_tensor("maskB", [6, 128], F32, kind="ExternalInput").ap()
    identb = nc.dram_tensor("identb", [128, 128], BF16, kind="ExternalInput").ap()
    onesb = nc.dram_tensor("onesb", [128, 1], BF16, kind="ExternalInput").ap()
    ones1 = nc.dram_tensor("ones1", [1, 128], F32, kind="ExternalInput").ap()
    cconst = nc.dram_tensor("cconst", [128, 3], F32, kind="ExternalInput").ap()
    yT = nc.dram_tensor("yT", [D, NTOK], F32, kind="ExternalOutput").ap()

    scw1 = nc.dram_tensor("scw1", [8 * 128 * DC, 512], BF16, kind="Internal").ap()
    scw2 = nc.dram_tensor("scw2", [DC * 128 * HC, 128], BF16, kind="Internal").ap()
    scqkv = nc.dram_tensor("scqkv", [24 * 128 * DC, 128], BF16, kind="Internal").ap()

    s_pe = SemC("s_pe")
    s_act = SemC("s_act")
    s_dve = SemC("s_dve")
    s_gpe = SemC("s_gpe")
    s_const = SemC("s_const")
    s_wload = [SemC(f"s_wl{i}") for i in range(4)]
    s_scwr = [SemC(f"s_scwr{i}") for i in range(4)]
    s_al = [SemC(f"s_al{i}") for i in range(4)]
    s_scq = [SemC(f"s_scq{i}") for i in range(4)]
    s_fw1 = [SemC("s_fw1a"), SemC("s_fw1b")]
    s_fw2 = [SemC("s_fw2a"), SemC("s_fw2b")]
    s_fqk = [SemC("s_fqka"), SemC("s_fqkb")]
    s_fvr = [SemC("s_fvra"), SemC("s_fvrb")]
    s_x = [SemC("s_x0"), SemC("s_x1")]
    s_out = SemC("s_out")
    sems = ([s_pe, s_act, s_dve, s_gpe, s_const, s_out]
            + s_wload + s_scwr + s_al + s_scq
            + s_fw1 + s_fw2 + s_fqk + s_fvr + s_x)
    rec = Rec(eng_sems={"scalar": s_act, "vector": s_dve, "gpsimd": s_gpe})

    with ExitStack() as ctx:
        sb = lambda n, s, d: ctx.enter_context(nc.sbuf_tensor(n, s, d))
        psa = lambda n, s, d: ctx.enter_context(nc.psum_tensor(n, s, d))

        xb = [sb("xb0", [128, DC * XE], F32), sb("xb1", [128, DC * XE], F32)]
        xn = sb("xn", [128, DC * MAXE], BF16)
        xcrot = sb("xcrot", [128, 4 * MAXE], BF16)
        sqrot = sb("sqrot", [128, 4 * MAXE], BF16)
        scr = [sb(f"scr{i}", [128, MAXE], F32) for i in range(3)]
        rows_t = sb("rows_t", [1, 2 * MAXE], F32)
        r1row = rows_t[0:1, 0:MAXE]
        r2row = rows_t[0:1, MAXE:2 * MAXE]
        g = sb("g", [128, HC * MAXE], BF16)
        qkrot = sb("qkrot", [128, 2 * MAXE], BF16)
        qphi = sb("qphi", [128, 4 * MAXE], BF16)
        kphi = sb("kphi", [128, 4 * MAXE], BF16)
        vtok = sb("vtok", [128, HEADS * 4 * VST], BF16)
        smA = sb("smA", [128, 2 * 128], BF16)
        smB = sb("smB", [6, 2 * 128], BF16)
        attn_n = sb("attn_n", [128, 2 * 128], BF16)
        attnT = sb("attnT", [128, HEADS * MAXE], BF16)
        zcol = sb("zcol", [128, 4], F32)
        w1s = sb("w1s", [128, 2 * 4096], BF16)
        w2s = sb("w2s", [128, 2 * 4096], BF16)
        qks = sb("qks", [128, 2 * 1024], BF16)
        vrs = sb("vrs", [128, 2 * 512], BF16)
        woutb = sb("woutb", [128, DC * 1024], BF16)
        projb = sb("projb", [128, FEAT], BF16)
        alpha1 = sb("alpha1", [128, HC], F32)
        alpha2 = sb("alpha2", [128, DC], F32)
        b1s = sb("b1s", [128, HC], F32)
        b2s = sb("b2s", [128, DC], F32)
        bouts = sb("bouts", [128, DC], F32)
        mA = sb("mA", [128, 3 * 128], F32)
        mB = sb("mB", [6, 128], F32)
        idb = sb("idb", [128, 128], BF16)
        onb = sb("onb", [128, 1], BF16)
        on1 = sb("on1", [1, 128], F32)
        ccs = sb("ccs", [128, 3], F32)
        pf32 = sb("pf32", [128, 4 * 1024], F32)
        pbf = sb("pbf", [128, 4 * 1024], BF16)
        pa = sb("pa", [128, 4 * 1024], F32)
        patm = sb("patm", [128, 4], F32)

        PB = [psa(f"PB{i}", [128, 512], F32) for i in range(6)]
        PT = [psa("PT6", [128, 1024], BF16), psa("PT7", [128, 1024], BF16)]

        for s in sems:
            s.h = ctx.enter_context(nc.semaphore(s.name))

        bank_free = {}

        def bank_wait(bid):
            w = bank_free.get(bid)
            return [w] if w else []

        def t_mm(out, lhsT, rhs, start, stop, waits=(), inc=None):
            rec.op("tensor",
                   lambda E, o=out, l=lhsT, r=rhs, st=start, sp=stop:
                   E.matmul(o, l, r, start=st, stop=sp),
                   waits=waits, inc=inc)

        def barrier():
            snap = [(s_pe, s_pe.n), (s_act, s_act.n), (s_dve, s_dve.n)]
            own = {"tensor": s_pe, "scalar": s_act, "vector": s_dve}
            for e in ("tensor", "scalar", "vector"):
                rec.op(e, None, waits=[(s, v) for (s, v) in snap if s is not own[e]])

        def gdma(sem, dst, src, waits=()):
            v = sem.inc(16)
            rec.op("gpsimdq", lambda E, d=dst, s=src: E.dma_start(out=d, in_=s),
                   waits=waits, inc=(sem, 16))
            return v

        def sdma(sem, dst, src, waits=()):
            v = sem.inc(16)
            rec.op("sync", lambda E, d=dst, s=src: E.dma_start(out=d, in_=s),
                   waits=waits, inc=(sem, 16))
            return v

        offs = [sum(TILES[:i]) for i in range(len(TILES))]
        xload_v = {}
        out_v = {}

        def load_x(i):
            buf = xb[i % 2]
            E = TILES[i] + 6
            waits = [] if i < 2 else [(s_out, out_v[i - 2])]
            sem = s_x[i % 2]
            for c in range(DC):
                gdma(sem, buf[:, c * XE: c * XE + E],
                     xT[c * 128:(c + 1) * 128, offs[i]: offs[i] + E],
                     waits=waits if c == 0 else ())
            xload_v[i] = (sem, sem.n)

        load_x(0)
        if len(TILES) > 1:
            load_x(1)

        gdma(s_const, mA[:], maskA3)
        gdma(s_const, mB[:], maskB)
        gdma(s_const, idb[:], identb)
        gdma(s_const, onb[:], onesb)
        gdma(s_const, on1[:], ones1)
        gdma(s_const, ccs[:], cconst)
        gdma(s_const, b1s[:], b1c)
        gdma(s_const, b2s[:], b2c)
        gdma(s_const, bouts[:], boutc)
        gdma(s_const, pa[:, 0:FEAT], proj)
        consts_v = s_const.n
        rec.op("gpsimd", lambda E: E.tensor_copy(projb[:], pa[:, 0:FEAT]),
               waits=[(s_const, consts_v)], inc=cinc(s_gpe))
        projb_v = s_gpe.n
        rec.op("gpsimd", lambda E: E.tensor_scalar_mul(b2s[:], b2s[:], 0.5),
               waits=[(s_const, consts_v)], inc=cinc(s_gpe))
        for h in range(HEADS):
            for tc in range(4):
                col = (h * 4 + tc) * VST + 128
                rec.op("gpsimd", lambda E, c=col: E.memset(vtok[:, c:c + 1], 1.0),
                       inc=cinc(s_gpe))
        v_ones = s_gpe.n

        scw1_g = scw1.rearrange("(g p k) c -> g p k c", p=128, k=DC)
        scw2_m = scw2.rearrange("(m p k) c -> m p k c", p=128, k=HC)
        scq_o = scqkv.rearrange("(o p k) c -> o p k c", p=128, k=DC)

        SLW = 1024
        slot_gate = [None] * 4
        wrf = [0] * 4

        def conv_step(it, load_src, consume_fn):
            sl = it % 4
            w = [slot_gate[sl]] if slot_gate[sl] else []
            v_l = sdma(s_wload[sl], pf32[:, sl * SLW:(sl + 1) * SLW], load_src,
                       waits=w)
            slot_gate[sl] = consume_fn(sl, [(s_wload[sl], v_l)])

        def sign_and_write(write_dst_fn):
            def consume(sl, waits):
                if wrf[sl]:
                    waits = waits + [(s_scwr[sl], wrf[sl])]
                rec.op("scalar",
                       lambda E, sl=sl: E.sign(pbf[:, sl * SLW:(sl + 1) * SLW],
                                               pf32[:, sl * SLW:(sl + 1) * SLW]),
                       waits=waits, inc=cinc(s_act))
                sv = s_act.n
                rec.op("scalarq", write_dst_fn(sl), waits=[(s_act, sv)],
                       inc=(s_scwr[sl], 16))
                wrf[sl] = s_scwr[sl].inc(16)
                return (s_act, sv)
            return consume

        w1grp_wait = {}
        for piece in range(4):
            for kc in range(DC):
                def _wr(sl, piece=piece, kc=kc):
                    return lambda E, sl=sl: E.dma_start(
                        out=scw1_g[piece * 2: piece * 2 + 2, :, kc, :].rearrange(
                            "g p c -> p g c"),
                        in_=pbf[:, sl * SLW:(sl + 1) * SLW].rearrange(
                            "p (g c) -> p g c", c=512))
                conv_step(piece * DC + kc,
                          w1t[kc * 128:(kc + 1) * 128,
                              piece * 1024:(piece + 1) * 1024],
                          sign_and_write(_wr))
            snap = [(s, s.n) for s in s_scwr]
            w1grp_wait[piece * 2] = snap
            w1grp_wait[piece * 2 + 1] = snap
        w1_scr_wait = w1grp_wait[7]

        a1col_v = [0] * HC
        a2col_v = [0] * DC
        deferred = {}
        aslot_gate = [None] * 4

        def alpha_step(it, load_src, reduce_fn):
            sl = it % 4
            w = [aslot_gate[sl]] if aslot_gate[sl] else []
            v_l = gdma(s_al[sl], pa[:, sl * SLW:(sl + 1) * SLW], load_src, waits=w)
            aslot_gate[sl] = reduce_fn(sl, [(s_al[sl], v_l)])

        def emit_deferred_prep():
            for hcn in range(HC):
                def _red(sl, waits, c=hcn):
                    rec.op("vector",
                           lambda E, sl=sl, c=c: E.tensor_reduce(
                               alpha1[:, c:c + 1], pa[:, sl * SLW:(sl + 1) * SLW],
                               axis=AXX, op=ALU.add, apply_absolute_value=True),
                           waits=waits, inc=cinc(s_dve))
                    rec.op("vector",
                           lambda E, c=c: E.tensor_scalar_mul(
                               alpha1[:, c:c + 1], alpha1[:, c:c + 1], 1.0 / D),
                           inc=cinc(s_dve))
                    a1col_v[c] = s_dve.n
                    return (s_dve, s_dve.n)
                alpha_step(hcn, w1n[hcn * 128:(hcn + 1) * 128, :], _red)

            for kc2 in range(HC):
                def _wr(sl, kc2=kc2):
                    return lambda E, sl=sl: E.dma_start(
                        out=scw2_m[:, :, kc2, :].rearrange("m p c -> p m c"),
                        in_=pbf[:, sl * SLW:(sl + 1) * SLW].rearrange(
                            "p (m c) -> p m c", c=128))
                conv_step(32 + kc2, w2t[kc2 * 128:(kc2 + 1) * 128, :],
                          sign_and_write(_wr))
            w2_scr_wait = [(s, s.n) for s in s_scwr]

            a2col_v.clear(); a2col_v.extend([0] * DC)
            for dcn in range(DC):
                for piece in range(4):
                    def _red(sl, waits, p=piece):
                        rec.op("vector",
                               lambda E, sl=sl, p=p: E.tensor_reduce(
                                   patm[:, p:p + 1], pa[:, sl * SLW:(sl + 1) * SLW],
                                   axis=AXX, op=ALU.add, apply_absolute_value=True),
                               waits=waits, inc=cinc(s_dve))
                        return (s_dve, s_dve.n)
                    alpha_step(32 + dcn * 4 + piece,
                               w2n[dcn * 128:(dcn + 1) * 128,
                                   piece * 1024:(piece + 1) * 1024], _red)
                rec.op("vector",
                       lambda E, c=dcn: E.tensor_reduce(
                           alpha2[:, c:c + 1], patm[:], axis=AXX, op=ALU.add),
                       inc=cinc(s_dve))
                rec.op("vector",
                       lambda E, c=dcn: E.tensor_scalar_mul(
                           alpha2[:, c:c + 1], alpha2[:, c:c + 1], 0.5 / H),
                       inc=cinc(s_dve))
                a2col_v[dcn] = s_dve.n

            wrq = [0] * 4
            def cast_and_write(write_dst_fn):
                def consume(sl, waits):
                    if wrf[sl]:
                        waits = waits + [(s_scwr[sl], wrf[sl])]
                    if wrq[sl]:
                        waits = waits + [(s_scq[sl], wrq[sl])]
                    rec.op("vector",
                           lambda E, sl=sl: E.tensor_copy(
                               pbf[:, sl * SLW:(sl + 1) * SLW],
                               pf32[:, sl * SLW:(sl + 1) * SLW]),
                           waits=waits, inc=cinc(s_dve))
                    cv = s_dve.n
                    rec.op("gpsimdq", write_dst_fn(sl), waits=[(s_dve, cv)],
                           inc=(s_scq[sl], 16))
                    wrq[sl] = s_scq[sl].inc(16)
                    return (s_dve, cv)
                return consume

            for kc in range(DC):
                for hh3 in range(3):
                    def _wr(sl, kc=kc, hh3=hh3):
                        return lambda E, sl=sl: E.dma_start(
                            out=scq_o[hh3 * 8:(hh3 + 1) * 8, :, kc, :].rearrange(
                                "o p c -> p o c"),
                            in_=pbf[:, sl * SLW:(sl + 1) * SLW].rearrange(
                                "p (o c) -> p o c", c=128))
                    conv_step(64 + kc * 3 + hh3,
                              wqkv[kc * 128:(kc + 1) * 128,
                                   hh3 * 1024:(hh3 + 1) * 1024],
                              cast_and_write(_wr))
            qkv_scr_wait = [(s, s.n) for s in s_scq]

            wout_done = 0
            for kc in range(DC):
                def _cast(sl, waits, c=kc):
                    rec.op("vector",
                           lambda E, sl=sl, c=c: E.tensor_copy(
                               woutb[:, c * 1024:(c + 1) * 1024],
                               pf32[:, sl * SLW:(sl + 1) * SLW]),
                           waits=waits, inc=cinc(s_dve))
                    return (s_dve, s_dve.n)
                conv_step(88 + kc, wout[kc * 128:(kc + 1) * 128, :], _cast)
            wout_done = s_dve.n
            deferred['wout_done'] = wout_done
            deferred['w2_scr'] = w2_scr_wait
            deferred['qkv_scr'] = qkv_scr_wait

        ln_state = {"xc": [None] * 4, "sq": [0] * 4, "xn_rd": 0}

        def ln_emit(buf, off, W):
            bsum, bsq, bcA, bcB = PB[0], PB[1], PB[2], PB[3]
            xc_free = ln_state["xc"]
            sq_free = ln_state["sq"]
            for c in range(DC):
                sl = c % 4
                w = list(xc_free[sl]) if xc_free[sl] else []
                rec.op("vector",
                       lambda E, c=c, sl=sl: E.tensor_copy(
                           xcrot[:, sl * MAXE: sl * MAXE + W],
                           buf[:, c * XE + off: c * XE + off + W]),
                       waits=w, inc=cinc(s_dve))
                cast_v = s_dve.n
                w = [(s_dve, cast_v)]
                if sq_free[sl]:
                    w.append((s_pe, sq_free[sl]))
                rec.op("scalar",
                       lambda E, sl=sl: E.square(sqrot[:, sl * MAXE: sl * MAXE + W],
                                                 xcrot[:, sl * MAXE: sl * MAXE + W]),
                       waits=w, inc=cinc(s_act))
                sq_v = s_act.n
                t_mm(bsum[0:1, 0:W], onb[:, 0:1], xcrot[:, sl * MAXE: sl * MAXE + W],
                     start=(c == 0), stop=(c == DC - 1),
                     waits=[(s_dve, cast_v)] + (bank_wait(0) if c == 0 else []),
                     inc=cinc(s_pe))
                xc_free[sl] = [(s_pe, s_pe.n), (s_act, sq_v)]
                t_mm(bsq[0:1, 0:W], onb[:, 0:1], sqrot[:, sl * MAXE: sl * MAXE + W],
                     start=(c == 0), stop=(c == DC - 1),
                     waits=[(s_act, sq_v)] + (bank_wait(1) if c == 0 else []),
                     inc=cinc(s_pe))
                sq_free[sl] = s_pe.n
            stats_mm = s_pe.n
            rec.op("scalar",
                   lambda E: E.activation(r1row[:, 0:W], bsum[0:1, 0:W],
                                          AF.Square, scale=ccs[0:1, 2:3]),
                   waits=[(s_pe, stats_mm)], inc=cinc(s_act))
            m2_v = s_act.n
            rec.op("vector",
                   lambda E: E.scalar_tensor_tensor(r2row[:, 0:W], bsq[0:1, 0:W],
                                                    1.0 / D, r1row[:, 0:W],
                                                    ALU.mult, ALU.subtract),
                   waits=[(s_pe, stats_mm), (s_act, m2_v)], inc=cinc(s_dve))
            var_v = s_dve.n
            rec.op("scalar",
                   lambda E: E.activation(r1row[:, 0:W], r2row[:, 0:W],
                                          AF.Ln, bias=ccs[0:1, 0:1]),
                   waits=[(s_dve, var_v)], inc=cinc(s_act))
            rec.op("scalar",
                   lambda E: E.activation(r2row[:, 0:W], r1row[:, 0:W],
                                          AF.Exp, scale=ccs[0:1, 1:2]),
                   inc=cinc(s_act))
            rstd_v = s_act.n
            rec.op("vector",
                   lambda E: E.scalar_tensor_tensor(r1row[:, 0:W], bsum[0:1, 0:W],
                                                    1.0 / D, r2row[:, 0:W],
                                                    ALU.mult, ALU.mult),
                   waits=[(s_act, rstd_v)], inc=cinc(s_dve))
            mr_v = s_dve.n
            t_mm(bcA[:, 0:W], on1[:], r2row[:, 0:W], True, True,
                 waits=[(s_act, rstd_v)] + bank_wait(2), inc=cinc(s_pe))
            t_mm(bcB[:, 0:W], on1[:], r1row[:, 0:W], True, True,
                 waits=[(s_dve, mr_v)] + bank_wait(3), inc=cinc(s_pe))
            bc_v = s_pe.n
            xn_vc = [0] * DC
            for c in range(DC):
                sl = c % 2
                rec.op("vector",
                       lambda E, c=c, sl=sl: E.tensor_tensor(
                           scr[sl][:, 0:W], buf[:, c * XE + off: c * XE + off + W],
                           bcA[:, 0:W], ALU.mult),
                       waits=[(s_pe, bc_v)] if c == 0 else (),
                       inc=cinc(s_dve))
                w2_ = [(s_pe, ln_state["xn_rd"])] if (c == 0 and ln_state["xn_rd"]) else []
                rec.op("vector",
                       lambda E, c=c, sl=sl: E.tensor_tensor(
                           xn[:, c * MAXE: c * MAXE + W], scr[sl][:, 0:W],
                           bcB[:, 0:W], ALU.subtract),
                       waits=w2_, inc=cinc(s_dve))
                xn_vc[c] = s_dve.n
            xn_v = s_dve.n
            bank_free[0] = (s_dve, mr_v)
            bank_free[1] = (s_dve, var_v)
            bank_free[2] = (s_dve, xn_v)
            bank_free[3] = (s_dve, xn_v)
            return xn_vc

        w1f = {"v": {0: 0, 1: 0}, "freed": {0: 0, 1: 0}}
        w2f = {"v": {0: 0, 1: 0}, "freed": {0: 0, 1: 0}}
        qkf = {"v": {0: 0, 1: 0}, "freed": {0: 0, 1: 0}}

        def ffn_emit(buf, off, W, first_tile, xn_vc):
            evac_last = 0
            for hc in range(HC):
                gidx, slot = hc // 4, (hc // 4) % 2
                if hc % 4 == 0:
                    w = list(w1grp_wait[gidx]) if first_tile else []
                    if w1f["freed"][slot]:
                        w.append((s_pe, w1f["freed"][slot]))
                    w1f["v"][slot] = sdma(
                        s_fw1[slot],
                        w1s[:, slot * 4096:(slot + 1) * 4096].rearrange(
                            "p (k c) -> p k c", c=512),
                        scw1_g[gidx], waits=w)
                bid = 2 + (hc % 4)
                for kc in range(DC):
                    waits = ([(s_fw1[slot], w1f["v"][slot])] + bank_wait(bid)) if kc == 0 else []
                    if hc == 0:
                        waits = waits + [(s_dve, xn_vc[kc])]
                    t_mm(PB[bid][:, 0:W],
                         w1s[:, slot * 4096 + kc * 512 + (hc % 4) * 128:
                             slot * 4096 + kc * 512 + (hc % 4) * 128 + 128],
                         xn[:, kc * MAXE: kc * MAXE + W],
                         start=(kc == 0), stop=(kc == DC - 1),
                         waits=waits, inc=cinc(s_pe) if kc == DC - 1 else None)
                mm_v = s_pe.n
                if hc % 4 == 3:
                    w1f["freed"][slot] = mm_v
                wg = [(s_dve, a1col_v[hc]), (s_pe, mm_v)]
                rec.op("scalar",
                       lambda E, hc=hc, bid=bid: E.activation(
                           g[:, hc * MAXE: hc * MAXE + W], PB[bid][:, 0:W], AF.Gelu,
                           scale=alpha1[:, hc:hc + 1], bias=b1s[:, hc:hc + 1]),
                       waits=wg,
                       inc=cinc(s_act))
                evac_last = s_act.n
                bank_free[bid] = (s_act, evac_last)
            ln_state["xn_rd"] = s_pe.n
            resid_free = {0: 0, 1: 0}
            for mc in range(DC):
                slot = mc % 2
                w = list(deferred['w2_scr']) if first_tile else []
                if w2f["freed"][slot]:
                    w.append((s_pe, w2f["freed"][slot]))
                w2f["v"][slot] = sdma(
                    s_fw2[slot],
                    w2s[:, slot * 4096:(slot + 1) * 4096].rearrange(
                        "p (k c) -> p k c", c=128),
                    scw2_m[mc], waits=w)
                bid = 2 + (mc % 4)
                bw = bank_wait(bid)
                for kc2 in range(HC):
                    waits = ([(s_fw2[slot], w2f["v"][slot]), (s_act, evac_last)] + bw) \
                        if kc2 == 0 else []
                    t_mm(PB[bid][:, 0:W],
                         w2s[:, slot * 4096 + kc2 * 128: slot * 4096 + kc2 * 128 + 128],
                         g[:, kc2 * MAXE: kc2 * MAXE + W],
                         start=(kc2 == 0), stop=(kc2 == HC - 1),
                         waits=waits, inc=cinc(s_pe) if kc2 == HC - 1 else None)
                mm_v = s_pe.n
                w2f["freed"][slot] = mm_v
                sl = mc % 2
                w = [(s_dve, a2col_v[mc]), (s_pe, mm_v)]
                if resid_free[sl]:
                    w.append((s_dve, resid_free[sl]))
                rec.op("scalar",
                       lambda E, bid=bid, sl=sl, mc=mc: E.activation(
                           scr[sl][:, 0:W], PB[bid][:, 0:W], AF.Identity,
                           scale=alpha2[:, mc:mc + 1], bias=b2s[:, mc:mc + 1]),
                       waits=w, inc=cinc(s_act))
                ev = s_act.n
                rec.op("vector",
                       lambda E, mc=mc, sl=sl: E.tensor_add(
                           buf[:, mc * XE + off: mc * XE + off + W],
                           buf[:, mc * XE + off: mc * XE + off + W],
                           scr[sl][:, 0:W]),
                       waits=[(s_act, ev)], inc=cinc(s_dve))
                resid_free[sl] = s_dve.n
                bank_free[bid] = (s_act, ev)

        def qkv_emit(E, first_tile, xn_vc):
            scrw = list(deferred['qkv_scr']) if first_tile else []
            qkrot_free = {0: 0, 1: 0}
            phi_free = {0: 0, 1: 0}
            scr_free = {0: 0, 1: 0}
            for oc in range(16):
                slot = oc % 2
                w = list(scrw)
                if qkf["freed"][slot]:
                    w.append((s_pe, qkf["freed"][slot]))
                qkf["v"][slot] = sdma(
                    s_fqk[slot],
                    qks[:, slot * 1024:(slot + 1) * 1024].rearrange(
                        "p (k c) -> p k c", c=128),
                    scq_o[oc], waits=w)
                bid = 2 + slot
                bw = bank_wait(bid)
                for kc in range(DC):
                    waits = ([(s_fqk[slot], qkf["v"][slot])] + bw) if kc == 0 else []
                    if oc == 0:
                        waits = waits + [(s_dve, xn_vc[kc])]
                    t_mm(PB[bid][:, 0:E],
                         qks[:, slot * 1024 + kc * 128: slot * 1024 + kc * 128 + 128],
                         xn[:, kc * MAXE: kc * MAXE + E],
                         start=(kc == 0), stop=(kc == DC - 1),
                         waits=waits, inc=cinc(s_pe) if kc == DC - 1 else None)
                mm_v = s_pe.n
                qkf["freed"][slot] = mm_v
                w = [(s_pe, mm_v)]
                if qkrot_free[slot]:
                    w.append((s_pe, qkrot_free[slot]))
                rec.op("scalar",
                       lambda E_, sl=slot, bid=bid: E_.copy(
                           qkrot[:, sl * MAXE: sl * MAXE + E], PB[bid][:, 0:E]),
                       waits=w, inc=cinc(s_act))
                evq = s_act.n
                bank_free[bid] = (s_act, evq)
                pbid = 4 + slot
                head = oc % 8
                w = [(s_act, evq)]
                if phi_free[slot]:
                    w.append((s_dve, phi_free[slot]))
                else:
                    w += bank_wait(pbid)
                w.append((s_gpe, projb_v))
                t_mm(PB[pbid][0:FEAT, 0:E], projb[:],
                     qkrot[:, slot * MAXE: slot * MAXE + E],
                     True, True, waits=w, inc=cinc(s_pe))
                phi_v = s_pe.n
                qkrot_free[slot] = phi_v
                rows = (head % 2) * 64
                w = [(s_pe, phi_v)]
                if scr_free[slot]:
                    w.append((s_dve, scr_free[slot]))
                rec.op("scalar",
                       lambda E_, pbid=pbid, rows=rows: E_.activation(
                           scr[0][rows:rows + 64, 0:E], PB[pbid][0:FEAT, 0:E], AF.Relu),
                       waits=w, inc=cinc(s_act))
                r_v = s_act.n
                rec.op("vector",
                       lambda E_, pbid=pbid, rows=rows: E_.tensor_sub(
                           scr[1][rows:rows + 64, 0:E], PB[pbid][0:FEAT, 0:E],
                           scr[0][rows:rows + 64, 0:E]),
                       waits=[(s_act, r_v)], inc=cinc(s_dve))
                m_v = s_dve.n
                phi_free[slot] = m_v
                rec.op("scalar",
                       lambda E_, rows=rows: E_.activation(
                           scr[2][rows:rows + 64, 0:E], scr[1][rows:rows + 64, 0:E],
                           AF.Exp),
                       waits=[(s_dve, m_v)], inc=cinc(s_act))
                e_v = s_act.n
                dst = qphi if oc < 8 else kphi
                pair = head // 2
                rec.op("vector",
                       lambda E_, dst=dst, pair=pair, rows=rows: E_.tensor_add(
                           dst[rows:rows + 64, pair * MAXE: pair * MAXE + E],
                           scr[2][rows:rows + 64, 0:E], scr[0][rows:rows + 64, 0:E]),
                       waits=[(s_act, e_v)], inc=cinc(s_dve))
                scr_free[slot] = s_dve.n
            bank_free[4] = (s_dve, phi_free[0])
            bank_free[5] = (s_dve, phi_free[1])
            nch = (E + 127) // 128
            vr_free = {0: 0, 1: 0}
            for h2 in range(2):
                vmm_v = 0
                for kc in range(DC):
                    slot = kc % 2
                    w = list(scrw)
                    if vr_free[slot]:
                        w.append((s_pe, vr_free[slot]))
                    v_l = sdma(s_fvr[slot],
                               vrs[:, slot * 512:(slot + 1) * 512].rearrange(
                                   "p (o c) -> p o c", c=128),
                               scq_o[16 + h2 * 4: 16 + h2 * 4 + 4, :, kc, :].rearrange(
                                   "o p c -> p o c"),
                               waits=w)
                    for tc in range(nch):
                        rows = min(128, E - tc * 128)
                        bid = 2 + tc
                        waits = [(s_fvr[slot], v_l)] + (bank_wait(bid) if kc == 0 else [])
                        t_mm(PB[bid][0:rows, 0:512],
                             xn[:, kc * MAXE + tc * 128: kc * MAXE + tc * 128 + rows],
                             vrs[:, slot * 512:(slot + 1) * 512],
                             start=(kc == 0), stop=(kc == DC - 1),
                             waits=waits,
                             inc=cinc(s_pe) if tc == nch - 1 else None)
                    vr_free[slot] = s_pe.n
                vmm_v = s_pe.n
                for tc in range(nch):
                    rows = min(128, E - tc * 128)
                    bid = 2 + tc
                    for hh in range(4):
                        head = h2 * 4 + hh
                        col = (head * 4 + tc) * VST
                        rec.op("scalar",
                               lambda E_, bid=bid, rows=rows, hh=hh, col=col: E_.copy(
                                   vtok[0:rows, col:col + 128],
                                   PB[bid][0:rows, hh * 128: hh * 128 + 128]),
                               waits=[(s_pe, vmm_v)], inc=cinc(s_act))
                    bank_free[bid] = (s_act, s_act.n)
            ln_state["xn_rd"] = s_pe.n

        def attn_emit(T, E, first, last):
            nqb = (T + 127) // 128
            sm_free = {0: 0, 1: 0}
            sAB_free = {0: 0, 1: 0}
            attn_free = {0: 0, 1: 0}
            an_free = {0: 0, 1: 0}
            pt_free = {0: 0, 1: 0}
            idx = 0
            for m in range(nqb):
                qb = min(128, T - m * 128)
                arows = min(128, E - m * 128)
                brows = max(0, qb - 122)
                mvar = 0 if (first and m == 0) else (2 if (last and m == nqb - 1) else 1)
                for head in range(HEADS):
                    slot = idx % 2
                    pair, rows = head // 2, (head % 2) * 64
                    w = [(s_dve, sAB_free[slot])] if sAB_free[slot] else []
                    t_mm(PB[0 + slot][0:arows, 0:qb],
                         kphi[rows:rows + 64,
                              pair * MAXE + m * 128: pair * MAXE + m * 128 + arows],
                         qphi[rows:rows + 64,
                              pair * MAXE + 3 + m * 128: pair * MAXE + 3 + m * 128 + qb],
                         True, True, waits=w,
                         inc=cinc(s_pe) if brows == 0 else None)
                    if brows > 0:
                        t_mm(PB[2 + slot][0:brows, 0:qb],
                             kphi[rows:rows + 64,
                                  pair * MAXE + (m + 1) * 128:
                                  pair * MAXE + (m + 1) * 128 + brows],
                             qphi[rows:rows + 64,
                                  pair * MAXE + 3 + m * 128:
                                  pair * MAXE + 3 + m * 128 + qb],
                             True, True, waits=(), inc=cinc(s_pe))
                    sc_v = s_pe.n
                    w = [(s_pe, sc_v)]
                    if sm_free[slot]:
                        w.append((s_pe, sm_free[slot]))
                    rec.op("vector",
                           lambda E_, slot=slot, arows=arows, qb=qb, mvar=mvar:
                           E_.tensor_tensor(
                               smA[0:arows, slot * 128: slot * 128 + qb],
                               PB[0 + slot][0:arows, 0:qb],
                               mA[0:arows, mvar * 128: mvar * 128 + qb], ALU.mult),
                           waits=w,
                           inc=cinc(s_dve) if brows == 0 else None)
                    if brows > 0:
                        rec.op("vector",
                               lambda E_, slot=slot, brows=brows, qb=qb:
                               E_.tensor_tensor(
                                   smB[0:brows, slot * 128: slot * 128 + qb],
                                   PB[2 + slot][0:brows, 0:qb],
                                   mB[0:brows, 0:qb], ALU.mult),
                               inc=cinc(s_dve))
                    sm_v = s_dve.n
                    sAB_free[slot] = sm_v
                    abid = 4 + slot
                    w = [(s_dve, sm_v), (s_gpe, v_ones)]
                    if attn_free[slot]:
                        w.append((s_act, attn_free[slot]))
                    t_mm(PB[abid][0:qb, 0:129],
                         smA[0:arows, slot * 128: slot * 128 + qb],
                         vtok[0:arows, (head * 4 + m) * VST: (head * 4 + m) * VST + 129],
                         True, (brows == 0), waits=w,
                         inc=cinc(s_pe) if brows == 0 else None)
                    if brows > 0:
                        t_mm(PB[abid][0:qb, 0:129],
                             smB[0:brows, slot * 128: slot * 128 + qb],
                             vtok[0:brows,
                                  (head * 4 + m + 1) * VST: (head * 4 + m + 1) * VST + 129],
                             False, True, waits=(), inc=cinc(s_pe))
                    at_v = s_pe.n
                    sm_free[slot] = at_v
                    rec.op("vector",
                           lambda E_, abid=abid, qb=qb, slot=slot: E_.tensor_scalar(
                               zcol[0:qb, 2 * slot: 2 * slot + 1],
                               PB[abid][0:qb, 128:129], EPS_Z, None, ALU.add),
                           waits=[(s_pe, at_v)])
                    rec.op("vector",
                           lambda E_, qb=qb, slot=slot: E_.reciprocal(
                               zcol[0:qb, 2 * slot + 1: 2 * slot + 2],
                               zcol[0:qb, 2 * slot: 2 * slot + 1]),
                           inc=cinc(s_dve))
                    zr_v = s_dve.n
                    w = [(s_dve, zr_v)]
                    if an_free[slot]:
                        w.append((s_pe, an_free[slot]))
                    rec.op("scalar",
                           lambda E_, abid=abid, qb=qb, slot=slot: E_.activation(
                               attn_n[0:qb, slot * 128: slot * 128 + 128],
                               PB[abid][0:qb, 0:128], AF.Identity,
                               scale=zcol[0:qb, 2 * slot + 1: 2 * slot + 2]),
                           waits=w, inc=cinc(s_act))
                    an_v = s_act.n
                    attn_free[slot] = an_v
                    w = [(s_act, an_v)]
                    if pt_free[slot]:
                        w.append((s_act, pt_free[slot]))
                    rec.op("tensor",
                           lambda E_, slot=slot, qb=qb: E_.transpose(
                               PT[slot][0:128, 0:qb],
                               attn_n[0:qb, slot * 128: slot * 128 + 128],
                               idb[0:qb, 0:qb]),
                           waits=w, inc=cinc(s_pe))
                    tr_v = s_pe.n
                    an_free[slot] = tr_v
                    rec.op("scalar",
                           lambda E_, slot=slot, head=head, m=m, qb=qb: E_.copy(
                               attnT[:, head * MAXE + m * 128: head * MAXE + m * 128 + qb],
                               PT[slot][0:128, 0:qb]),
                           waits=[(s_pe, tr_v)], inc=cinc(s_act))
                    pt_free[slot] = s_act.n
                    idx += 1

        def outproj_emit(buf, T):
            resid_free = {0: 0, 1: 0}
            for mc in range(DC):
                bid = 2 + (mc % 4)
                bw = bank_wait(bid)
                for kc in range(DC):
                    waits = ([(s_dve, deferred['wout_done'])] + bw) if kc == 0 else []
                    t_mm(PB[bid][:, 0:T],
                         woutb[:, kc * 1024 + mc * 128: kc * 1024 + mc * 128 + 128],
                         attnT[:, kc * MAXE: kc * MAXE + T],
                         start=(kc == 0), stop=(kc == DC - 1),
                         waits=waits, inc=cinc(s_pe) if kc == DC - 1 else None)
                mm_v = s_pe.n
                sl = mc % 2
                w = [(s_pe, mm_v)]
                if resid_free[sl]:
                    w.append((s_dve, resid_free[sl]))
                rec.op("scalar",
                       lambda E_, bid=bid, sl=sl, mc=mc, T=T: E_.activation(
                           scr[sl][:, 0:T], PB[bid][:, 0:T], AF.Identity,
                           bias=bouts[:, mc:mc + 1]),
                       waits=w, inc=cinc(s_act))
                ev = s_act.n
                rec.op("vector",
                       lambda E_, mc=mc, sl=sl, T=T: E_.tensor_add(
                           buf[:, mc * XE + 3: mc * XE + 3 + T],
                           buf[:, mc * XE + 3: mc * XE + 3 + T], scr[sl][:, 0:T]),
                       waits=[(s_act, ev)], inc=cinc(s_dve))
                resid_free[sl] = s_dve.n
                bank_free[bid] = (s_act, ev)

        for i, T in enumerate(TILES):
            E = T + 6
            buf = xb[i % 2]
            first_tile = (i == 0)
            for e in ("tensor", "scalar", "vector"):
                rec.op(e, None, waits=[xload_v[i], (s_const, consts_v)])
            xn_vc = ln_emit(buf, 0, E)
            if i == 0:
                emit_deferred_prep()
            ffn_emit(buf, 0, E, first_tile, xn_vc)
            if stage >= 2:
                xn_vc = ln_emit(buf, 0, E)
                qkv_emit(E, first_tile, xn_vc)
                barrier()
                attn_emit(T, E, first=(i == 0), last=(i == len(TILES) - 1))
                barrier()
                outproj_emit(buf, T)
            if stage >= 3:
                xn_vc = ln_emit(buf, 3, T)
                ffn_emit(buf, 3, T, False, xn_vc)
            store_wait = [(s_dve, s_dve.n)]
            for c in range(DC):
                gdma(s_out, yT[c * 128:(c + 1) * 128, offs[i]: offs[i] + T],
                     buf[:, c * XE + 3: c * XE + 3 + T],
                     waits=store_wait if c == 0 else ())
            out_v[i] = s_out.n
            if i + 2 < len(TILES):
                load_x(i + 2)

        rec.op("gpsimd", None, waits=[(s_out, out_v[len(TILES) - 1])])

        block = ctx.enter_context(nc.Block())

        def emit_for(name):
            extra = {"gpsimd": {"gpsimdq"}, "scalar": {"scalarq"}}
            names = {name} | extra.get(name, set())

            def emitter(engobj):
                for eng, fn, waits, incs in rec.ops:
                    if eng not in names:
                        continue
                    for s, v in waits:
                        engobj.wait_ge(s.h, v)
                    if fn is None:
                        if incs:
                            inst = engobj.nop()
                            for s, amt in incs:
                                inst.then_inc(s.h, amt)
                        continue
                    inst = fn(engobj)
                    for s, amt in incs:
                        inst.then_inc(s.h, amt)
            return emitter

        block.tensor(emit_for("tensor"))
        block.scalar(emit_for("scalar"))
        block.vector(emit_for("vector"))
        block.sync(emit_for("sync"))
        block.gpsimd(emit_for("gpsimd"))

    return nc


def _host_inputs(L0_attn_x, L0_attn_w_qkv, L0_attn_proj, L0_attn_w_out, L0_attn_b_out,
                 L0_ffn_w1, L0_ffn_b1, L0_ffn_w2, L0_ffn_b2):
    x = np.asarray(L0_attn_x, dtype=np.float32)
    wqkv = np.ascontiguousarray(np.asarray(L0_attn_w_qkv, np.float32))
    projm = np.ascontiguousarray(np.asarray(L0_attn_proj, np.float32))
    woutm = np.ascontiguousarray(np.asarray(L0_attn_w_out, np.float32))
    bout = np.asarray(L0_attn_b_out, np.float32)
    w1 = np.asarray(L0_ffn_w1, np.float32)
    b1 = np.asarray(L0_ffn_b1, np.float32)
    w2 = np.asarray(L0_ffn_w2, np.float32)
    b2 = np.asarray(L0_ffn_b2, np.float32)

    w1t = np.ascontiguousarray(w1.T)
    w2t = np.ascontiguousarray(w2.T)
    b1cm = np.ascontiguousarray(b1.reshape(HC, 128).T)
    b2cm = np.ascontiguousarray(b2.reshape(DC, 128).T)
    boutcm = np.ascontiguousarray(bout.reshape(DC, 128).T)
    identb = np.eye(128, dtype=ml_dtypes.bfloat16)
    onesb = np.ones((128, 1), dtype=ml_dtypes.bfloat16)
    ones1 = np.ones((1, 128), dtype=np.float32)
    cconst = np.broadcast_to(np.array([EPS_LN, -0.5, 1.0 / D], np.float32),
                             (128, 3)).copy()

    j = np.arange(128)[:, None]
    q = np.arange(128)[None, :]
    bandA = ((j - q >= 0) & (j - q <= 6)).astype(np.float32)
    jb = np.arange(6)[:, None]
    bandB = (q >= 122 + jb).astype(np.float32)
    T_last = TILES[-1]
    cut_last = (T_last + 3) - 384

    in_maps = []
    for core in range(8):
        b = core // 4
        start = (core % 4) * NTOK
        lo, hi = start - 3, start + NTOK + 3
        xs = np.zeros((NTOK + 6, D), np.float32)
        clo, chi = max(lo, 0), min(hi, 16384)
        xs[clo - lo: chi - lo, :] = x[b, clo:chi, :]
        xTm = np.ascontiguousarray(xs.T)

        Afirst = bandA.copy()
        Alast = bandA.copy()
        if start == 0:
            Afirst[0:3, :] = 0.0
        if start + NTOK == 16384:
            Alast[cut_last:cut_last + 3, :] = 0.0
        maskA3m = np.ascontiguousarray(np.concatenate([Afirst, bandA, Alast], axis=1))

        in_maps.append({
            "xT": xTm, "wqkv": wqkv, "wout": woutm, "proj": projm,
            "w1t": w1t, "w1n": w1, "w2t": w2t, "w2n": w2,
            "b1c": b1cm, "b2c": b2cm, "boutc": boutcm,
            "maskA3": maskA3m, "maskB": np.ascontiguousarray(bandB),
            "identb": identb, "onesb": onesb, "ones1": ones1, "cconst": cconst,
        })
    return in_maps


def kernel(L0_attn_x, L0_attn_w_qkv, L0_attn_proj, L0_attn_w_out, L0_attn_b_out,
           L0_ffn_w1, L0_ffn_b1, L0_ffn_w2, L0_ffn_b2):
    in_maps = _host_inputs(L0_attn_x, L0_attn_w_qkv, L0_attn_proj, L0_attn_w_out,
                           L0_attn_b_out, L0_ffn_w1, L0_ffn_b1, L0_ffn_w2, L0_ffn_b2)
    if "nc" not in _CACHED:
        _CACHED["nc"] = build_nc(STAGE)
    nc = _CACHED["nc"]
    res = run_bass_kernel_spmd(nc, in_maps, core_ids=list(range(8)))
    out = np.empty((2, 16384, D), np.float32)
    for core in range(8):
        b = core // 4
        start = (core % 4) * NTOK
        out[b, start:start + NTOK, :] = res.results[core]["yT"].T
    return out
